# version 19
# baseline (speedup 1.0000x reference)
import numpy as np
from contextlib import ExitStack

import concourse.bass as bass
import concourse.mybir as mybir
from concourse.bass_utils import run_bass_kernel_spmd

F32 = mybir.dt.float32
BF16 = mybir.dt.bfloat16
AF = mybir.ActivationFunctionType
ALU = mybir.AluOpType
AXX = mybir.AxisListType.X

S = 4096
E = 1024
HF = 4096
NP = 8
NG = 4
NQC = 8
NKC = 32
NEC = 8
NMC = 8
NHFC = 32
NST = 8
EPS = 1e-5
SCALE = 0.125

ENGINES = ("pe", "act", "dve", "sp", "gp")


class Tracer:

    def __init__(self, nc, stack):
        self.nc = nc
        self.stack = stack
        self.streams = {e: [] for e in ENGINES}
        self.semh = {}
        self.semv = {}
        for name in ("PE", "ACT", "DVE"):
            self._mk(name)

    def _mk(self, name):
        self.semh[name] = self.stack.enter_context(self.nc.semaphore(name))
        self.semv[name] = 0

    def val(self, sem):
        return self.semv[sem]

    def wait(self, stream, sem, value):
        if value <= 0 or sem not in self.semh:
            return
        h = self.semh[sem]
        self.streams[stream].append(lambda e, h=h, v=value: e.wait_ge(h, v))

    AUTO_CHAIN = {"dve": "DVE", "act": "ACT"}

    def op(self, stream, fn, sem=None, n=1):
        ch = self.AUTO_CHAIN.get(stream)
        if ch is not None:
            self.wait(stream, ch, self.semv[ch])
            if sem is None:
                sem, n = ch, 1
            assert sem == ch
        if sem is None:
            self.streams[stream].append(fn)
            return None
        if sem not in self.semh:
            self._mk(sem)
        self.semv[sem] += n
        h = self.semh[sem]
        self.streams[stream].append(
            lambda e, fn=fn, h=h, n=n: fn(e).then_inc(h, n))
        return self.semv[sem]


def build(debug=(), phases="AB", nqc=NQC, nst=NST, sparts="LTAV"):
    alvl = 9
    sub4 = "c"
    if phases.startswith("A") and len(phases) >= 2 and phases[1].isdigit():
        alvl = int(phases[1])
        if len(phases) > 2:
            sub4 = phases[2]
    nc = bass.Bass("TRN2", target_bir_lowering=False)

    x_d = nc.dram_tensor("x", [S, E], F32, kind="ExternalInput").ap()
    wq_d = nc.dram_tensor("wq", [E, E], F32, kind="ExternalInput").ap()
    wk_d = nc.dram_tensor("wk", [E, E], F32, kind="ExternalInput").ap()
    wv_d = nc.dram_tensor("wv", [E, E], F32, kind="ExternalInput").ap()
    wo_d = nc.dram_tensor("wo", [E, E], F32, kind="ExternalInput").ap()
    bq_d = nc.dram_tensor("bq", [1, E], F32, kind="ExternalInput").ap()
    bk_d = nc.dram_tensor("bk", [1, E], F32, kind="ExternalInput").ap()
    bv_d = nc.dram_tensor("bv", [1, E], F32, kind="ExternalInput").ap()
    bo_d = nc.dram_tensor("bo", [1, E], F32, kind="ExternalInput").ap()
    w1_d = nc.dram_tensor("w1", [E, HF], F32, kind="ExternalInput").ap()
    b1_d = nc.dram_tensor("b1", [1, HF], F32, kind="ExternalInput").ap()
    w2_d = nc.dram_tensor("w2", [HF, E], F32, kind="ExternalInput").ap()
    b2_d = nc.dram_tensor("b2", [1, E], F32, kind="ExternalInput").ap()
    idm_d = nc.dram_tensor("idm", [128, 128], F32, kind="ExternalInput").ap()
    y_d = nc.dram_tensor("y", [S, E], F32, kind="ExternalOutput").ap()

    def scratch(name, shape, dtype):
        kind = "ExternalOutput" if name in debug else "Internal"
        return nc.dram_tensor(name, shape, dtype, kind=kind).ap()

    xt_d = scratch("xt_f", [E, S], F32)
    q_d = scratch("q_scr", [E, S], BF16)
    k_d = scratch("k_scr", [NP * 128, S], BF16)
    v_d = scratch("v_scr", [NP * 128, S], BF16)
    w1s_d = scratch("w1_scr", [NHFC * NEC * 128, 128], BF16)
    w2s_d = scratch("w2_scr", [NMC * NHFC * 128, 128], BF16)
    wos_d = scratch("wo_scr", [NMC * NEC * 128, 128], BF16)

    with ExitStack() as st0:
        tr = Tracer(nc, st0)
        SB = lambda *a: st0.enter_context(nc.sbuf_tensor(*a))
        PS = lambda *a: st0.enter_context(nc.psum_tensor(*a))
        pe, act, dve = "PE", "ACT", "DVE"

        ones_b1 = SB("ones_b1", [1, 128], BF16)
        ones_f1 = SB("ones_f1", [1, 128], F32)
        ones32 = SB("ones32", [128, 32], BF16)
        ones_c1f = SB("ones_c1f", [128, 1], F32)
        id_f = SB("id_f", [128, 128], F32)
        bq_pp = SB("bq_pp", [128, NEC], F32)
        bk_pp = SB("bk_pp", [128, NEC], F32)
        bo_pp = SB("bo_pp", [128, NEC], F32)
        b2_pp = SB("b2_pp", [128, NEC], F32)
        b1_pp = SB("b1_pp", [128, NHFC], F32)
        kap_pp = SB("kap_pp", [128, NEC], F32)
        xmean = SB("xmean", [128, NEC], F32)
        xpart2 = SB("xpart2", [128, NEC * NST], F32)
        xmean_b = SB("xmean_b", [128, NEC], BF16)
        bv_row = SB("bv_row", [1, E], BF16)

        tr.op("dve", lambda e: e.memset(ones_b1[:], 1.0))
        tr.op("dve", lambda e: e.memset(ones_f1[:], 1.0))
        tr.op("dve", lambda e: e.memset(ones32[:], 1.0))
        v_ones = tr.op("dve", lambda e: e.memset(ones_c1f[:], 1.0), sem=dve)
        v_id = tr.op("sp", lambda e: e.dma_start(id_f[:], idm_d), sem="s_id", n=16)
        v_bv = tr.op("gp", lambda e: e.dma_start(bv_row[:], bv_d), sem="g_bv", n=16)

        with ExitStack() as stA:
            SBA = lambda *a: stA.enter_context(nc.sbuf_tensor(*a))
            PSA = lambda *a: stA.enter_context(nc.psum_tensor(*a))

            xt_bf = [SBA(f"xt_bf{i}", [128, S], BF16) for i in range(NEC)]
            wq_sb = SBA("wq_sb", [128, NEC * E], BF16)
            wk_sb = SBA("wk_sb", [128, NEC * E], BF16)
            wv_sb = SBA("wv_sb", [128, NEC * E], BF16)
            wcast = [SBA(f"wcast{i}", [128, HF], BF16) for i in range(2)]
            xnat = [SBA(f"xnat{i}", [128, 4 * E], F32) for i in range(2)]
            xtf_b = [SBA(f"xtf_b{i}", [128, 512], F32) for i in range(2)]
            q_b = [SBA(f"q_b{i}", [128, S], BF16) for i in range(2)]
            v_b = [SBA(f"v_b{i}", [128, E], BF16) for i in range(2)]
            brow = SBA("brow", [128, HF], F32)

            pst = [PSA(f"pst{i}", [128, 512], F32) for i in range(2)]
            psq = [PSA(f"psq{i}", [128, 512], F32) for i in range(2)]
            psv = [PSA(f"psv{i}", [128, E], F32) for i in range(2)]

            tr.wait("pe", dve, v_ones)
            tr.wait("pe", "s_id", v_id)
            last_bias_dve = 0
            for bd, pp_t, nch in ((bq_d, bq_pp, NEC), (bk_d, bk_pp, NEC),
                                  (bo_d, bo_pp, NEC), (b2_d, b2_pp, NEC),
                                  (b1_d, b1_pp, NHFC)):
                tr.wait("sp", dve, last_bias_dve)
                v_ld = tr.op("sp", lambda e, bd=bd, n=nch: e.dma_start(
                    brow[0:1, 0:128 * n], bd), sem="s_brow", n=16)
                tr.wait("pe", "s_brow", v_ld)
                for j4 in range(0, nch, 4):
                    nb = min(4, nch - j4)
                    v_tp = None
                    for jj in range(nb):
                        j = j4 + jj
                        fn = lambda e, j=j, jj=jj: e.transpose(
                            pst[0][:, jj * 128:(jj + 1) * 128],
                            brow[:, j * 128:(j + 1) * 128], id_f[:])
                        v_tp = tr.op("pe", fn, sem=(pe if jj == nb - 1 else None))
                    tr.wait("dve", pe, v_tp)
                    last_bias_dve = tr.op(
                        "dve", lambda e, pp_t=pp_t, j4=j4, nb=nb: e.tensor_copy(
                            pp_t[:, j4:j4 + nb],
                            bass.AP(pst[0], 0, [[512, 128], [128, nb]])),
                        sem=dve)
                    tr.wait("pe", dve, last_bias_dve)

            v_wc = 0
            wcast_list = () if alvl < 2 else ((wq_d, wq_sb), (wk_d, wk_sb), (wv_d, wv_sb))
            for w_src, w_dst in wcast_list:
                v_wc = tr.op("gp", lambda e, s=w_src, d=w_dst: e.dma_start(
                    bass.AP(d, 0, [[NEC * E, 128], [E, NEC], [1, E]]),
                    bass.AP(s.tensor, 0, [[E, 128], [128 * E, NEC], [1, E]])),
                    sem="g_wc", n=16)

            def emit_jobs():
                prev_wr = {0: 0, 1: 0}
                jobs = ([("w1", cc, w1_d[cc * 128:(cc + 1) * 128, :], HF) for cc in range(NEC)]
                    + [("w2", hc, w2_d[hc * 128:(hc + 1) * 128, :], E) for hc in range(NHFC)]
                    + [("wo", cc, wo_d[cc * 128:(cc + 1) * 128, :], E) for cc in range(NEC)])
                if alvl < 3:
                    jobs = []
                v_scrw = 0
                for ji, (kind, idx, src, width) in enumerate(jobs):
                    buf = ji % 2
                    tr.wait("gp", f"s_jw{buf}", prev_wr[buf])
                    v_cast = tr.op("gp", lambda e, b=buf, s=src, w=width: e.dma_start(
                        wcast[b][:, 0:w], s), sem=f"g_jc{buf}", n=16)
                    tr.wait("sp", f"g_jc{buf}", v_cast)
                    if kind == "w1":
                        for h8 in range(0, NHFC, 8):
                            d8 = bass.AP(w1s_d.tensor,
                                         (h8 * NEC + idx) * 128 * 128,
                                         [[128, 128], [NEC * 128 * 128, 8], [1, 128]])
                            s8 = bass.AP(wcast[buf], h8 * 128,
                                         [[HF, 128], [128, 8], [1, 128]])
                            v_scrw = tr.op("sp", lambda e, d8=d8, s8=s8: e.dma_start(
                                d8, s8), sem=f"s_jw{buf}", n=16)
                    elif kind == "w2":
                        for m8 in range(0, NMC, 8):
                            d8 = bass.AP(w2s_d.tensor,
                                         (m8 * NHFC + idx) * 128 * 128,
                                         [[128, 128], [NHFC * 128 * 128, 8], [1, 128]])
                            s8 = bass.AP(wcast[buf], m8 * 128,
                                         [[HF, 128], [128, 8], [1, 128]])
                            v_scrw = tr.op("sp", lambda e, d8=d8, s8=s8: e.dma_start(
                                d8, s8), sem=f"s_jw{buf}", n=16)
                    else:
                        d8 = bass.AP(wos_d.tensor, idx * 128 * 128,
                                     [[128, 128], [NEC * 128 * 128, NMC], [1, 128]])
                        s8 = bass.AP(wcast[buf], 0,
                                     [[HF, 128], [128, NMC], [1, 128]])
                        v_scrw = tr.op("sp", lambda e, d8=d8, s8=s8: e.dma_start(
                            d8, s8), sem=f"s_jw{buf}", n=16)
                    prev_wr[buf] = v_scrw
                    yield
            jobs_gen = emit_jobs()
            stripe_pe_done = [0, 0]
            pst_free_d = [0, 0]
            pst_free_a = [0, 0]
            psq_free = [0, 0]
            psv_free = [0, 0]
            qk_buf_free = [0, 0]
            v_buf_free = [0, 0]
            xtf_free = [0, 0]

            xpart = xmean
            v_xtbf_l = [0] * (NST + 1)

            v_xld_l = [0] * (NST + 1)

            def emit_xload(stp):
                buf = stp % 2
                tr.wait("sp", pe, stripe_pe_done[buf])
                if "L" in sparts:
                    v_xld_l[stp] = tr.op("sp", lambda e, stp=stp, b=buf: e.dma_start(
                        bass.AP(xnat[b], 0, [[4 * E, 128], [E, 4], [1, E]]),
                        bass.AP(x_d.tensor, stp * 512 * E,
                                [[E, 128], [128 * E, 4], [1, E]])),
                        sem=f"s_x{buf}", n=16)

            def emit_transposes(stp):
                buf = stp % 2
                tr.wait("pe", f"s_x{buf}", v_xld_l[stp])
                for ec in range(NEC if "T" in sparts else 0):
                    tb = ec % 2
                    tr.wait("pe", act, pst_free_a[tb])
                    v_tp = None
                    for j in range(4):
                        fn = lambda e, b=buf, ec=ec, j=j, tb=tb: e.transpose(
                            pst[tb][:, j * 128:(j + 1) * 128],
                            xnat[b][:, j * E + ec * 128: j * E + (ec + 1) * 128],
                            id_f[:])
                        v_tp = tr.op("pe", fn, sem=(pe if j == 3 else None))
                    if "V" in sparts:
                        tr.wait("act", pe, v_tp)
                        tr.wait("act", f"s_xtf{tb}", xtf_free[tb])
                        va = tr.op("act", lambda e, tb=tb: e.copy(
                            xtf_b[tb][:], pst[tb][:]))
                        pst_free_a[tb] = va
                        if "A" in sparts:
                            tr.wait("dve", act, va)
                            vv = tr.op("dve", lambda e, ec=ec, stp=stp, tb=tb: e.tensor_copy(
                                xt_bf[ec][:, stp * 512:(stp + 1) * 512],
                                xtf_b[tb][:]), sem=dve)
                        else:
                            vv = tr.val(dve)
                        if alvl >= 5 and sub4 >= "b":
                            tr.op("dve", lambda e, ec=ec, stp=stp: e.reduce_sum(
                                xpart2[:, ec * NST + stp: ec * NST + stp + 1],
                                xt_bf[ec][:, stp * 512:(stp + 1) * 512], axis=AXX))
                        tr.wait("sp", act, va)
                        tr.wait("sp", dve, vv)
                        xtf_free[tb] = tr.op("sp", lambda e, ec=ec, stp=stp, tb=tb: e.dma_start(
                            xt_d[ec * 128:(ec + 1) * 128, stp * 512:(stp + 1) * 512],
                            xtf_b[tb][:]), sem=f"s_xtf{tb}", n=16)
                    else:
                        pst_free_a[tb] = tr.val(act)
                        vv = tr.val(dve)
                v_xtbf_l[stp] = tr.val(dve)
                stripe_pe_done[buf] = tr.val(pe)

            def emit_qv(stp):
                if sub4 >= "b":
                    if stp == 0:
                        tr.wait("pe", "g_wc", v_wc)
                        tr.wait("pe", "g_bv", v_bv)
                    tr.wait("pe", dve, v_xtbf_l[stp])
                sb_big = q_b[stp % 2]
                if sub4 >= "b":
                    tr.wait("dve", f"s_qk{stp % 2}", qk_buf_free[stp % 2])
                vv = 0
                for fc in range(NEC if sub4 >= "b" else 0):
                    qb = fc % 2
                    tr.wait("pe", dve, psq_free[qb])
                    v_mm = None
                    for cc in range(NEC):
                        fn = lambda e, fc=fc, cc=cc, stp=stp, qb=qb: e.matmul(
                            psq[qb][:],
                            wq_sb[:, cc * E + fc * 128: cc * E + (fc + 1) * 128],
                            xt_bf[cc][:, stp * 512:(stp + 1) * 512],
                            start=(cc == 0), stop=(cc == NEC - 1))
                        v_mm = tr.op("pe", fn, sem=(pe if cc == NEC - 1 else None))
                    tr.wait("dve", pe, v_mm)
                    vv = tr.op("dve", lambda e, fc=fc, qb=qb, sb_big=sb_big: e.tensor_scalar(
                        sb_big[:, fc * 512:(fc + 1) * 512], psq[qb][:],
                        bq_pp[:, fc:fc + 1], None, ALU.add), sem=dve)
                    psq_free[qb] = vv
                if sub4 >= "b":
                    tr.wait("sp", dve, vv)
                    qk_buf_free[stp % 2] = tr.op(
                        "sp", lambda e, stp=stp, sb_big=sb_big: e.dma_start(
                            bass.AP(q_d.tensor, stp * 512,
                                    [[S, 128], [128 * S, NEC], [1, 512]]),
                            bass.AP(sb_big, 0, [[S, 128], [512, NEC], [1, 512]])),
                        sem=f"s_qk{stp % 2}", n=16)

                for j in range(4 if sub4 >= "c" else 0):
                    vb = j % 2
                    kc = stp * 4 + j
                    tr.wait("pe", dve, psv_free[vb])
                    v_mm = None
                    for half in range(2):
                        for cc in range(NEC):
                            tr.op("pe", lambda e, j=j, cc=cc, stp=stp, vb=vb, half=half: e.matmul(
                                psv[vb][:, half * 512:(half + 1) * 512],
                                xt_bf[cc][:, stp * 512 + j * 128: stp * 512 + (j + 1) * 128],
                                wv_sb[:, cc * E + half * 512: cc * E + (half + 1) * 512],
                                start=(cc == 0), stop=False))
                        fn = lambda e, vb=vb, half=half: e.matmul(
                            psv[vb][:, half * 512:(half + 1) * 512],
                            ones_b1[:], bv_row[:, half * 512:(half + 1) * 512],
                            start=False, stop=True)
                        v_mm = tr.op("pe", fn, sem=(pe if half == 1 else None))
                    tr.wait("dve", pe, v_mm)
                    tr.wait("dve", f"g_vst{vb}", v_buf_free[vb])
                    vv = tr.op("dve", lambda e, vb=vb: e.tensor_copy(
                        v_b[vb][:], psv[vb][:]), sem=dve)
                    psv_free[vb] = vv
                    tr.wait("gp", dve, vv)
                    v_buf_free[vb] = tr.op("gp", lambda e, kc=kc, vb=vb: e.dma_start(
                        bass.AP(v_d.tensor, kc * 128,
                                [[S, 128], [128 * S, NP], [1, 128]]),
                        bass.AP(v_b[vb], 0, [[E, 128], [128, NP], [1, 128]])),
                        sem=f"g_vst{vb}", n=16)

            nst_eff = nst if alvl >= 4 else 0
            for stp in range(nst_eff + 1):
                if stp < nst_eff:
                    emit_xload(stp)
                if stp >= 1:
                    emit_qv(stp - 1)
                    for _ in range(3):
                        try:
                            next(jobs_gen)
                        except StopIteration:
                            break
                if stp < nst_eff:
                    emit_transposes(stp)
                    for _ in range(4):
                        try:
                            next(jobs_gen)
                        except StopIteration:
                            break

            v_xm = 0
            if alvl >= 5:
                for ec in range(NEC):
                    tr.op("dve", lambda e, ec=ec: e.reduce_sum(
                        xmean[:, ec:ec + 1],
                        xpart2[:, ec * NST:(ec + 1) * NST], axis=AXX))
                v_xm = tr.op("dve", lambda e: e.tensor_scalar(
                    xmean_b[:], xmean[:], 1.0 / S, None, ALU.mult), sem=dve)

            tr.wait("pe", dve, v_xm)
            tr.wait("pe", act, pst_free_a[0])
            v_kap = 0
            for fc in range(NEC if alvl >= 5 else 0):
                v_mm = None
                for cc in range(NEC):
                    fn = lambda e, fc=fc, cc=cc: e.matmul(
                        psq[0][:, 0:1],
                        wk_sb[:, cc * E + fc * 128: cc * E + (fc + 1) * 128],
                        xmean_b[:, cc:cc + 1],
                        start=(cc == 0), stop=(cc == NEC - 1))
                    v_mm = tr.op("pe", fn, sem=(pe if cc == NEC - 1 else None))
                tr.wait("dve", pe, v_mm)
                v_kap = tr.op("dve", lambda e, fc=fc: e.tensor_scalar(
                    kap_pp[:, fc:fc + 1], psq[0][:, 0:1], bk_pp[:, fc:fc + 1],
                    None, ALU.add), sem=dve)
                tr.wait("pe", dve, v_kap)

            for stp in range(NST if alvl >= 5 else 0):
                sb_big = q_b[stp % 2]
                tr.wait("dve", f"s_qk{stp % 2}", qk_buf_free[stp % 2])
                vv = 0
                for fc in range(NEC):
                    qb = fc % 2
                    tr.wait("pe", dve, psq_free[qb])
                    v_mm = None
                    for cc in range(NEC):
                        fn = lambda e, fc=fc, cc=cc, stp=stp, qb=qb: e.matmul(
                            psq[qb][:],
                            wk_sb[:, cc * E + fc * 128: cc * E + (fc + 1) * 128],
                            xt_bf[cc][:, stp * 512:(stp + 1) * 512],
                            start=(cc == 0), stop=(cc == NEC - 1))
                        v_mm = tr.op("pe", fn, sem=(pe if cc == NEC - 1 else None))
                    tr.wait("dve", pe, v_mm)
                    vv = tr.op("dve", lambda e, fc=fc, qb=qb, sb_big=sb_big: e.tensor_scalar(
                        sb_big[:, fc * 512:(fc + 1) * 512], psq[qb][:],
                        bk_pp[:, fc:fc + 1],
                        kap_pp[:, fc:fc + 1], ALU.add, ALU.subtract), sem=dve)
                    psq_free[qb] = vv
                tr.wait("sp", dve, vv)
                qk_buf_free[stp % 2] = tr.op(
                    "sp", lambda e, stp=stp, sb_big=sb_big: e.dma_start(
                        bass.AP(k_d.tensor, stp * 512,
                                [[S, 128], [128 * S, NEC], [1, 512]]),
                        bass.AP(sb_big, 0, [[S, 128], [512, NEC], [1, 512]])),
                    sem=f"s_qk{stp % 2}", n=16)

            for _ in jobs_gen:
                pass
            A_pe, A_act, A_dve = tr.val(pe), tr.val(act), tr.val(dve)
            A_finals = {s: tr.val(s) for s in ("s_qk0", "s_qk1", "s_xtf0",
                                               "s_xtf1", "g_vst0", "g_vst1",
                                               "s_jw0", "s_jw1", "s_x0",
                                               "s_x1", "s_brow", "g_wc")
                        if s in tr.semv}

        k_sb = [SB(f"k_sb{i}", [128, S], BF16) for i in range(2)]
        v_sb = [SB(f"v_sb{i}", [128, S], BF16) for i in range(2)]
        q_sb = SB("q_sb", [128, S], BF16)
        xt_sb = [SB(f"xt_sb{i}", [128, 512], F32) for i in range(2)]
        wo_sb = [SB(f"wo_sb{i}", [128, E], BF16) for i in range(2)]
        w1_sb = [SB(f"w1_sb{i}", [128, E], BF16) for i in range(3)]
        w2_sb = [SB(f"w2_sb{i}", [128, HF // 2], BF16) for i in range(3)]
        es_sb = [SB(f"es_sb{i}", [128, 1024], BF16) for i in range(2)]
        o_sb = [SB(f"o_sb{i}", [128, S], BF16) for i in range(2)]
        hpre_sb = SB("hpre_sb", [128, 16 * 512], BF16)
        ones64 = SB("ones64", [128, 64], BF16)
        y1_sb = SB("y1_sb", [128, S], F32)
        h1b_sb = SB("h1b_sb", [128, S], BF16)
        h1f_sb = SB("h1f_sb", [128, S], F32)
        h2_sb = SB("h2_sb", [128, NHFC * 512], BF16)
        out_sb = [SB("out_sb0", [128, E], F32)]
        su_sb = [SB(f"su_sb{i}", [128, 1024], F32) for i in range(2)]
        sq_sb = SB("sq_sb", [128, 512], F32)
        stA = SB("stA", [128, 512], F32)
        stB = SB("stB", [128, 512], F32)
        rb_sb = SB("rb_sb", [128, 512], F32)
        mrb_sb = SB("mrb_sb", [128, 512], F32)
        r_sb = SB("r_sb", [128, 512], F32)
        dD_sb = SB("dD_sb", [128, 512], F32)
        oO_sb = SB("oO_sb", [128, 512], F32)

        ps_s = [PS("ps_s0", [128, 1024], F32), PS("ps_s1", [128, 1024], F32)]
        ps_o = PS("ps_o", [128, 512], F32)
        ps_d = PS("ps_d", [128, 512], F32)
        ps_g = PS("ps_g", [128, 512], F32)
        ps_w = PS("ps_w", [128, 512], F32)
        pwb = [ps_w, ps_g]

        v_ones64 = tr.op("dve", lambda e: e.memset(ones64[:], 1.0))

        for q_ in ("sp", "gp"):
            tr.wait(q_, pe, A_pe)
            tr.wait(q_, act, A_act)
            tr.wait(q_, dve, A_dve)
            for s_, v_ in A_finals.items():
                tr.wait(q_, s_, v_)

        free_k = [0, 0]
        free_v = [0, 0]
        free_q = 0
        free_xt = [0, 0]
        free_wo = [0, 0]
        free_w1 = [0, 0, 0]
        free_w2 = [0, 0, 0]
        free_out = [0]
        v_k = [0, 0]
        v_v = [0, 0]
        v_q = 0
        v_xt = [0, 0]
        v_wo = [0, 0]
        v_w1 = [0, 0, 0]
        v_w2 = [0, 0, 0]
        es_free = {}
        norm_done = 0
        o_last = 0
        o_ready = {}
        last_pw = [0, 0]
        ln_app = 0
        gelu_batch = {}
        qc_range = range(nqc) if "B" in phases else range(0)

        def load_kv(p):
            b = p % 2
            tr.wait("gp", pe, free_k[b])
            v_k[b] = tr.op("gp", lambda e, p=p, b=b: e.dma_start(
                k_sb[b][:], bass.AP(k_d.tensor, p * 128 * S, [[S, 128], [1, S]])),
                sem=f"s_k{b}", n=16)
            tr.wait("gp", pe, free_v[b])
            v_v[b] = tr.op("gp", lambda e, p=p, b=b: e.dma_start(
                v_sb[b][:], bass.AP(v_d.tensor, p * 128 * S, [[S, 128], [1, S]])),
                sem=f"s_v{b}", n=16)

        def stats_chunk(src_sb, c, st):
            tr.op("dve", lambda e, c=c: e.tensor_tensor(
                sq_sb[:], src_sb[:, c * 512:(c + 1) * 512],
                src_sb[:, c * 512:(c + 1) * 512], ALU.mult))
            if c == 0:
                tr.op("dve", lambda e: e.tensor_copy(
                    su_sb[0][:, 0:512], src_sb[:, 0:512]))
                st["v"] = tr.op("dve", lambda e: e.tensor_copy(
                    su_sb[0][:, 512:1024], sq_sb[:]))
                st["pp"] = 0
            else:
                pp = st["pp"]
                np_ = 1 - pp
                tr.op("dve", lambda e, c=c, pp=pp, np_=np_: e.tensor_tensor(
                    su_sb[np_][:, 0:512], su_sb[pp][:, 0:512],
                    src_sb[:, c * 512:(c + 1) * 512], ALU.add))
                st["v"] = tr.op("dve", lambda e, pp=pp, np_=np_: e.tensor_tensor(
                    su_sb[np_][:, 512:1024], su_sb[pp][:, 512:1024],
                    sq_sb[:], ALU.add))
                st["pp"] = np_

        def emit_ln_tail(src_sb, dst_f, dst_b, st):
            nonlocal ln_app
            pp = st["pp"]
            v_sums = st["v"]
            tr.wait("pe", dve, max(v_sums, last_pw[0], last_pw[1]))
            tr.op("pe", lambda e, pp=pp: e.matmul(
                pwb[0][0:1, :], ones_c1f[:], su_sb[pp][:, 0:512],
                start=True, stop=True))
            v_st = tr.op("pe", lambda e, pp=pp: e.matmul(
                pwb[1][0:1, :], ones_c1f[:], su_sb[pp][:, 512:1024],
                start=True, stop=True), sem=pe)
            tr.wait("dve", pe, v_st)
            last_pw[0] = tr.op("dve", lambda e: e.tensor_copy(stA[0:1, :], pwb[0][0:1, :]))
            last_pw[1] = tr.op("dve", lambda e: e.tensor_copy(stA[32:33, :], pwb[1][0:1, :]))
            tr.op("dve", lambda e: e.tensor_scalar(
                stB[0:1, :], stA[0:1, :], 1.0 / E, None, ALU.mult))
            tr.op("dve", lambda e: e.tensor_tensor(
                stB[32:33, :], stB[0:1, :], stB[0:1, :], ALU.mult))
            tr.op("dve", lambda e: e.scalar_tensor_tensor(
                stA[64:65, :], stA[32:33, :], 1.0 / E,
                stB[32:33, :], ALU.mult, ALU.subtract))
            v_var = tr.op("dve", lambda e: e.tensor_scalar(
                stA[96:97, :], stA[64:65, :], EPS, None, ALU.add))
            yield
            yield
            tr.wait("act", dve, v_var)
            tr.op("act", lambda e: e.activation(
                stB[96:97, :], stA[96:97, :], AF.Ln))
            v_rs = tr.op("act", lambda e: e.activation(
                stA[0:1, :], stB[96:97, :], AF.Exp, scale=-0.5))
            tr.wait("dve", act, v_rs)
            v_mr = tr.op("dve", lambda e: e.tensor_tensor(
                sq_sb[0:1, :], stB[0:1, :], stA[0:1, :], ALU.mult))
            yield
            yield
            tr.wait("pe", dve, max(v_mr, last_pw[0], last_pw[1]))
            tr.op("pe", lambda e: e.matmul(
                pwb[0][:], ones_f1[:], stA[0:1, :], start=True, stop=True))
            v_bc = tr.op("pe", lambda e: e.matmul(
                pwb[1][:], ones_f1[:], sq_sb[0:1, :], start=True, stop=True), sem=pe)
            tr.wait("dve", pe, v_bc)
            last_pw[0] = tr.op("dve", lambda e: e.tensor_copy(rb_sb[:], pwb[0][:]))
            last_pw[1] = tr.op("dve", lambda e: e.tensor_copy(mrb_sb[:], pwb[1][:]))
            yield
            yield
            for c in range(NMC):
                tr.op("dve", lambda e, c=c: e.tensor_tensor(
                    sq_sb[:], src_sb[:, c * 512:(c + 1) * 512], rb_sb[:],
                    ALU.mult))
                if dst_f is not None:
                    tr.op("dve", lambda e, c=c: e.tensor_tensor(
                        dst_f[:, c * 512:(c + 1) * 512], sq_sb[:], mrb_sb[:],
                        ALU.subtract))
                    ln_app = tr.op("dve", lambda e, c=c: e.tensor_copy(
                        dst_b[:, c * 512:(c + 1) * 512],
                        dst_f[:, c * 512:(c + 1) * 512]))
                else:
                    ln_app = tr.op("dve", lambda e, c=c: e.tensor_tensor(
                        dst_b[:, c * 512:(c + 1) * 512], sq_sb[:], mrb_sb[:],
                        ALU.subtract))
                yield

        def make_post(pq):
            nonlocal ln_app, norm_done
            st1 = {"pp": 0, "v": 0}
            st2 = {"pp": 0, "v": 0}
            osrc = o_sb[pq % 2]
            for mc in range(NMC):
                wb = mc % 2
                xb = mc % 2
                tr.wait("gp", dve, free_xt[xb])
                v_xt[xb] = tr.op("gp", lambda e, mc=mc, pq=pq, xb=xb: e.dma_start(
                    xt_sb[xb][:],
                    bass.AP(xt_d.tensor, mc * 128 * S + pq * 512, [[S, 128], [1, 512]])),
                    sem=f"s_xt{xb}", n=16)
                tr.wait("sp", pe, free_wo[wb])
                v_wo[wb] = tr.op("sp", lambda e, mc=mc, wb=wb: e.dma_start(
                    bass.AP(wo_sb[wb], 0, [[E, 128], [128, NEC], [1, 128]]),
                    bass.AP(wos_d.tensor, mc * NEC * 128 * 128,
                            [[128, 128], [128 * 128, NEC], [1, 128]])),
                    sem=f"s_wo{wb}", n=16)
                tr.wait("pe", f"s_wo{wb}", v_wo[wb])
                pb = mc % 2
                tr.wait("pe", dve, max(last_pw[pb], o_ready.get(pq, 0)))
                v_mm = None
                for cc in range(NEC):
                    fn = lambda e, cc=cc, wb=wb, pb=pb: e.matmul(
                        pwb[pb][:], wo_sb[wb][:, cc * 128:(cc + 1) * 128],
                        osrc[:, cc * 512:(cc + 1) * 512],
                        start=(cc == 0), stop=(cc == NEC - 1))
                    v_mm = tr.op("pe", fn, sem=(pe if cc == NEC - 1 else None))
                free_wo[wb] = v_mm
                tr.wait("dve", pe, v_mm)
                tr.wait("dve", f"s_xt{xb}", v_xt[xb])
                last_pw[pb] = tr.op("dve", lambda e, mc=mc, xb=xb, pb=pb: e.scalar_tensor_tensor(
                    y1_sb[:, mc * 512:(mc + 1) * 512], pwb[pb][:],
                    bo_pp[:, mc:mc + 1], xt_sb[xb][:],
                    ALU.add, ALU.add))
                free_xt[xb] = last_pw[pb]
                stats_chunk(y1_sb, mc, st1)
                yield
            yield from emit_ln_tail(y1_sb, h1f_sb, h1b_sb, st1)
            yield
            yield
            yield
            yield
            v_ln1 = ln_app
            tr.wait("pe", dve, v_ln1)
            for hc in range(NHFC):
                wb = hc % 3
                pb = hc % 2
                tr.wait("sp", pe, free_w1[wb])
                v_w1[wb] = tr.op("sp", lambda e, hc=hc, wb=wb: e.dma_start(
                    bass.AP(w1_sb[wb], 0, [[E, 128], [128, NEC], [1, 128]]),
                    bass.AP(w1s_d.tensor, hc * NEC * 128 * 128,
                            [[128, 128], [128 * 128, NEC], [1, 128]])),
                    sem=f"s_w1{wb}", n=16)
                tr.wait("pe", f"s_w1{wb}", v_w1[wb])
                tr.wait("pe", dve, last_pw[pb])
                v_mm = None
                for cc in range(NEC):
                    fn = lambda e, cc=cc, wb=wb, pb=pb: e.matmul(
                        pwb[pb][:], w1_sb[wb][:, cc * 128:(cc + 1) * 128],
                        h1b_sb[:, cc * 512:(cc + 1) * 512],
                        start=(cc == 0), stop=(cc == NEC - 1))
                    v_mm = tr.op("pe", fn, sem=(pe if cc == NEC - 1 else None))
                free_w1[wb] = v_mm
                tr.wait("dve", pe, v_mm)
                if hc < 16:
                    tr.wait("dve", act, gelu_batch.get((pq - 1, 1), 0))
                else:
                    tr.wait("dve", act, gelu_batch.get((pq, 0), 0))
                last_pw[pb] = tr.op("dve", lambda e, hc=hc, pb=pb: e.tensor_scalar(
                    hpre_sb[:, (hc % 16) * 512:(hc % 16 + 1) * 512], pwb[pb][:],
                    b1_pp[:, hc:hc + 1], None, ALU.add))
                if hc % 16 == 15:
                    bi = hc // 16
                    tr.wait("act", dve, max(last_pw[0], last_pw[1]))
                    gelu_batch[(pq, bi)] = tr.op(
                        "act", lambda e, hc=hc: e.activation(
                            h2_sb[:, (hc - 15) * 512:(hc + 1) * 512],
                            hpre_sb[:], AF.Gelu))
                yield
            v_gelu = gelu_batch[(pq, 1)]
            ffn2_ev = {}
            for mc in range(NMC):
                bufs = [(2 * mc) % 3, (2 * mc + 1) % 3]
                for half in range(2):
                    wb = bufs[half]
                    tr.wait("sp", pe, free_w2[wb])
                    v_w2[wb] = tr.op("sp", lambda e, mc=mc, wb=wb, half=half: e.dma_start(
                        bass.AP(w2_sb[wb], 0, [[HF // 2, 128], [128, 16], [1, 128]]),
                        bass.AP(w2s_d.tensor, (mc * NHFC + half * 16) * 128 * 128,
                                [[128, 128], [128 * 128, 16], [1, 128]])),
                        sem=f"s_w2{wb}", n=16)
                tr.wait("pe", act, v_gelu)
                pb = mc % 2
                tr.wait("pe", dve, last_pw[pb])
                v_mm = None
                for sub in range(4):
                    wb = bufs[sub // 2]
                    if sub % 2 == 0:
                        tr.wait("pe", f"s_w2{wb}", v_w2[wb])
                    for hi in range(8):
                        hc = sub * 8 + hi
                        fn = lambda e, hc=hc, wb=wb, pb=pb: e.matmul(
                            pwb[pb][:], w2_sb[wb][:, (hc % 16) * 128:(hc % 16 + 1) * 128],
                            h2_sb[:, hc * 512:(hc + 1) * 512],
                            start=(hc == 0), stop=(hc == NHFC - 1))
                        v_mm = tr.op("pe", fn,
                                     sem=(pe if hc == NHFC - 1 or hi == 7 else None))
                    if sub == 1:
                        free_w2[bufs[0]] = v_mm
                    if sub < 3:
                        yield
                free_w2[bufs[1]] = v_mm
                tr.wait("dve", pe, v_mm)
                last_pw[pb] = tr.op("dve", lambda e, mc=mc, pb=pb: e.scalar_tensor_tensor(
                    y1_sb[:, mc * 512:(mc + 1) * 512], pwb[pb][:],
                    b2_pp[:, mc:mc + 1], h1f_sb[:, mc * 512:(mc + 1) * 512],
                    ALU.add, ALU.add))
                stats_chunk(y1_sb, mc, st2)
                yield
            yield from emit_ln_tail(y1_sb, None, h1f_sb, st2)
            yield
            v_ln2 = ln_app
            for ts_ in range(4):
                for half in range(2):
                    tr.wait("pe", dve, max(v_ln2, last_pw[half]))
                    if half == 0:
                        tr.wait("pe", "g_out", free_out[0])
                    v_tp = None
                    for mi in range(4):
                        mc = half * 4 + mi
                        fn = lambda e, ts_=ts_, mc=mc, mi=mi, half=half: e.transpose(
                            pwb[half][:, mi * 128:(mi + 1) * 128],
                            h1f_sb[:, mc * 512 + ts_ * 128: mc * 512 + (ts_ + 1) * 128],
                            id_f[:])
                        v_tp = tr.op("pe", fn, sem=(pe if mi == 3 else None))
                    tr.wait("dve", pe, v_tp)
                    last_pw[half] = tr.op("dve", lambda e, half=half: e.tensor_copy(
                        out_sb[0][:, half * 512:(half + 1) * 512], pwb[half][:]))
                tr.wait("sp", dve, max(last_pw[0], last_pw[1]))
                free_out[0] = tr.op("sp", lambda e, ts_=ts_, pq=pq: e.dma_start(
                    y_d[pq * 512 + ts_ * 128: pq * 512 + (ts_ + 1) * 128, :],
                    out_sb[0][:]), sem="g_out", n=16)
                yield

        post_gen = None

        def pump(n=1):
            nonlocal post_gen
            for _ in range(n):
                if post_gen is None:
                    return
                try:
                    next(post_gen)
                except StopIteration:
                    post_gen = None
                    return

        def load_q(qc):
            nonlocal v_q
            tr.wait("gp", pe, free_q)
            for pc in range(NP):
                v_q = tr.op("gp", lambda e, pc=pc, qc=qc: e.dma_start(
                    q_sb[:, pc * 512:(pc + 1) * 512],
                    bass.AP(q_d.tensor, pc * 128 * S + qc * 512, [[S, 128], [1, 512]])),
                    sem="s_qb", n=16)

        for qc in qc_range:
            post_gen = make_post(qc - 1) if qc > 0 else None
            if qc == 0:
                tr.wait("pe", dve, v_ones64)
                load_q(0)
                load_kv(0)
                load_kv(1)
            for p in range(NP):
                b = p % 2
                if 1 <= p < NP - 1:
                    load_kv(p + 1)
                elif p == NP - 1 and qc + 1 in qc_range:
                    load_kv(0)
                sc_done = {}
                exp_done = {}

                def scores(kc, p=p, b=b):
                    if kc == 0:
                        tr.wait("pe", f"s_k{b}", v_k[b])
                        tr.wait("pe", "s_qb", v_q)
                    v = None
                    for hh in range(2):
                        fn = lambda e, p=p, b=b, kc=kc, hh=hh: e.matmul(
                            ps_s[kc % 2][:, hh * 512:(hh + 1) * 512],
                            k_sb[b][hh * 64:(hh + 1) * 64, kc * 128:(kc + 1) * 128],
                            q_sb[hh * 64:(hh + 1) * 64, p * 512:(p + 1) * 512],
                            start=True, stop=True, tile_position=(hh * 64, 0))
                        v = tr.op("pe", fn, sem=(pe if hh == 1 else None))
                    sc_done[kc] = v

                def do_exp(kc):
                    esb = kc % 2
                    tr.wait("act", pe, es_free.get(esb, 0))
                    tr.wait("act", pe, sc_done[kc])
                    exp_done[kc] = tr.op(
                        "act", lambda e, kc=kc, esb=esb: e.activation(
                            es_sb[esb][:], ps_s[kc % 2][:], AF.Exp, scale=SCALE))

                def pv(kc, p=p, b=b):
                    tr.wait("pe", act, exp_done[kc])
                    if kc == 0:
                        tr.wait("pe", f"s_v{b}", v_v[b])
                        tr.wait("pe", dve, norm_done)
                    esb = kc % 2
                    for hh in range(2):
                        tr.op("pe", lambda e, b=b, kc=kc, esb=esb, hh=hh: e.matmul(
                            ps_o[hh * 64:(hh + 1) * 64, :],
                            v_sb[b][:, kc * 128 + hh * 64: kc * 128 + (hh + 1) * 64],
                            es_sb[esb][:, hh * 512:(hh + 1) * 512],
                            start=(kc == 0), stop=(kc == NKC - 1),
                            skip_group_check=True,
                            tile_position=(0, hh * 64)))

                def denom(kc):
                    esb = kc % 2
                    v = None
                    for hh in range(2):
                        fn = lambda e, kc=kc, esb=esb, hh=hh: e.matmul(
                            ps_d[hh * 64:(hh + 1) * 64, :],
                            ones64[:],
                            es_sb[esb][:, hh * 512:(hh + 1) * 512],
                            start=(kc == 0), stop=(kc == NKC - 1),
                            skip_group_check=True,
                            tile_position=(0, hh * 64))
                        v = tr.op("pe", fn, sem=(pe if hh == 1 else None))
                    return v

                scores(0)
                do_exp(0)
                scores(1)
                vmm = 0
                for kc in range(NKC):
                    pv(kc)
                    vmm = denom(kc)
                    es_free[kc % 2] = vmm
                    if kc + 2 < NKC:
                        scores(kc + 2)
                    if kc + 1 < NKC:
                        do_exp(kc + 1)
                    if kc % 2 == 1:
                        pump(2 if p < 4 else 1)
                    if kc == 3:
                        pump(2)
                free_k[b] = vmm
                free_v[b] = vmm
                tr.wait("dve", pe, vmm)
                tr.op("dve", lambda e: e.tensor_copy(dD_sb[:], ps_d[:]))
                norm_done = tr.op("dve", lambda e: e.tensor_copy(oO_sb[:], ps_o[:]))
                tr.op("dve", lambda e: e.reciprocal(r_sb[:], dD_sb[:]))
                pcol = p * 512
                tr.op("dve", lambda e, pcol=pcol, qc=qc: e.tensor_tensor(
                    o_sb[qc % 2][0:64, pcol:pcol + 512],
                    oO_sb[0:64, :], r_sb[0:64, :], ALU.mult))
                o_last = tr.op("dve", lambda e, pcol=pcol, qc=qc: e.tensor_tensor(
                    o_sb[qc % 2][64:128, pcol:pcol + 512],
                    oO_sb[64:128, :], r_sb[64:128, :], ALU.mult))
                if p == NP - 1:
                    free_q = vmm
                pump(1)
            o_ready[qc] = o_last
            if qc + 1 in qc_range:
                load_kv(1)
                load_q(qc + 1)
            while post_gen is not None:
                pump(4)
        if "B" in phases and nqc > 0:
            post_gen = make_post(nqc - 1)
            while post_gen is not None:
                pump(8)

        tr.wait("sp", "g_out", tr.val("g_out") if "g_out" in tr.semv else 0)

        with nc.Block() as block:
            @block.tensor
            def _(eng):
                for f in tr.streams["pe"]:
                    f(eng)

            @block.scalar
            def _(eng):
                for f in tr.streams["act"]:
                    f(eng)

            @block.vector
            def _(eng):
                for f in tr.streams["dve"]:
                    f(eng)

            @block.sync
            def _(eng):
                for f in tr.streams["sp"]:
                    f(eng)

            @block.gpsimd
            def _(eng):
                for f in tr.streams["gp"]:
                    f(eng)

    return nc


_ID = np.eye(128, dtype=np.float32)


def kernel(L0_attn_x, L0_attn_Wq, L0_attn_bq, L0_attn_Wk, L0_attn_bk,
           L0_attn_Wv, L0_attn_bv, L0_attn_Wo, L0_attn_bo,
           L0_ffn_w1, L0_ffn_b1, L0_ffn_w2, L0_ffn_b2):
    x = np.ascontiguousarray(np.asarray(L0_attn_x, dtype=np.float32))
    f = lambda a: np.ascontiguousarray(np.asarray(a, dtype=np.float32))
    row = lambda a: f(a).reshape(1, -1)
    nc = build()
    shared = {
        "wq": f(L0_attn_Wq), "wk": f(L0_attn_Wk), "wv": f(L0_attn_Wv),
        "wo": f(L0_attn_Wo), "bq": row(L0_attn_bq), "bk": row(L0_attn_bk),
        "bv": row(L0_attn_bv), "bo": row(L0_attn_bo),
        "w1": f(L0_ffn_w1), "b1": row(L0_ffn_b1),
        "w2": f(L0_ffn_w2), "b2": row(L0_ffn_b2), "idm": _ID,
    }
    in_maps = [dict(shared, x=np.ascontiguousarray(x[c])) for c in range(8)]
    res = run_bass_kernel_spmd(nc, in_maps, core_ids=list(range(8)))
    return np.stack([np.asarray(r["y"]) for r in res.results], axis=0)


# version 20
# speedup vs baseline: 1.0991x; 1.0991x over previous
import numpy as np
from contextlib import ExitStack

import concourse.bass as bass
import concourse.mybir as mybir
from concourse.bass_utils import run_bass_kernel_spmd

F32 = mybir.dt.float32
BF16 = mybir.dt.bfloat16
AF = mybir.ActivationFunctionType
ALU = mybir.AluOpType
AXX = mybir.AxisListType.X

S = 4096
E = 1024
HF = 4096
NP = 8
NG = 4
NQC = 8
NKC = 32
NEC = 8
NMC = 8
NHFC = 32
NST = 8
EPS = 1e-5
SCALE = 0.125

ENGINES = ("pe", "act", "dve", "sp", "gp")


class Tracer:

    def __init__(self, nc, stack):
        self.nc = nc
        self.stack = stack
        self.streams = {e: [] for e in ENGINES}
        self.semh = {}
        self.semv = {}
        for name in ("PE", "ACT", "DVE"):
            self._mk(name)

    def _mk(self, name):
        self.semh[name] = self.stack.enter_context(self.nc.semaphore(name))
        self.semv[name] = 0

    def val(self, sem):
        return self.semv[sem]

    def wait(self, stream, sem, value):
        if value <= 0 or sem not in self.semh:
            return
        h = self.semh[sem]
        self.streams[stream].append(lambda e, h=h, v=value: e.wait_ge(h, v))

    AUTO_CHAIN = {"dve": "DVE", "act": "ACT"}

    def op(self, stream, fn, sem=None, n=1):
        ch = self.AUTO_CHAIN.get(stream)
        if ch is not None:
            self.wait(stream, ch, self.semv[ch])
            if sem is None:
                sem, n = ch, 1
            assert sem == ch
        if sem is None:
            self.streams[stream].append(fn)
            return None
        if sem not in self.semh:
            self._mk(sem)
        self.semv[sem] += n
        h = self.semh[sem]
        self.streams[stream].append(
            lambda e, fn=fn, h=h, n=n: fn(e).then_inc(h, n))
        return self.semv[sem]


def build(debug=(), phases="AB", nqc=NQC, nst=NST, sparts="LTAV"):
    alvl = 9
    sub4 = "c"
    if phases.startswith("A") and len(phases) >= 2 and phases[1].isdigit():
        alvl = int(phases[1])
        if len(phases) > 2:
            sub4 = phases[2]
    nc = bass.Bass("TRN2", target_bir_lowering=False)

    x_d = nc.dram_tensor("x", [S, E], F32, kind="ExternalInput").ap()
    wq_d = nc.dram_tensor("wq", [E, E], F32, kind="ExternalInput").ap()
    wk_d = nc.dram_tensor("wk", [E, E], F32, kind="ExternalInput").ap()
    wv_d = nc.dram_tensor("wv", [E, E], F32, kind="ExternalInput").ap()
    wo_d = nc.dram_tensor("wo", [E, E], F32, kind="ExternalInput").ap()
    bq_d = nc.dram_tensor("bq", [1, E], F32, kind="ExternalInput").ap()
    bk_d = nc.dram_tensor("bk", [1, E], F32, kind="ExternalInput").ap()
    bv_d = nc.dram_tensor("bv", [1, E], F32, kind="ExternalInput").ap()
    bo_d = nc.dram_tensor("bo", [1, E], F32, kind="ExternalInput").ap()
    w1_d = nc.dram_tensor("w1", [E, HF], F32, kind="ExternalInput").ap()
    b1_d = nc.dram_tensor("b1", [1, HF], F32, kind="ExternalInput").ap()
    w2_d = nc.dram_tensor("w2", [HF, E], F32, kind="ExternalInput").ap()
    b2_d = nc.dram_tensor("b2", [1, E], F32, kind="ExternalInput").ap()
    idm_d = nc.dram_tensor("idm", [128, 128], F32, kind="ExternalInput").ap()
    y_d = nc.dram_tensor("y", [S, E], F32, kind="ExternalOutput").ap()

    def scratch(name, shape, dtype):
        kind = "ExternalOutput" if name in debug else "Internal"
        return nc.dram_tensor(name, shape, dtype, kind=kind).ap()

    xt_d = scratch("xt_f", [E, S], F32)
    q_d = scratch("q_scr", [E, S], BF16)
    k_d = scratch("k_scr", [NP * 128, S], BF16)
    v_d = scratch("v_scr", [NP * 128, S], BF16)
    w1s_d = scratch("w1_scr", [NHFC * NEC * 128, 128], BF16)
    w2s_d = scratch("w2_scr", [NMC * NHFC * 128, 128], BF16)
    wos_d = scratch("wo_scr", [NMC * NEC * 128, 128], BF16)

    with ExitStack() as st0:
        tr = Tracer(nc, st0)
        SB = lambda *a: st0.enter_context(nc.sbuf_tensor(*a))
        PS = lambda *a: st0.enter_context(nc.psum_tensor(*a))
        pe, act, dve = "PE", "ACT", "DVE"

        ones_b1 = SB("ones_b1", [1, 128], BF16)
        ones_f1 = SB("ones_f1", [1, 128], F32)
        ones32 = SB("ones32", [128, 32], BF16)
        ones_c1f = SB("ones_c1f", [128, 1], F32)
        id_f = SB("id_f", [128, 128], F32)
        bq_pp = SB("bq_pp", [128, NEC], F32)
        bk_pp = SB("bk_pp", [128, NEC], F32)
        bo_pp = SB("bo_pp", [128, NEC], F32)
        b2_pp = SB("b2_pp", [128, NEC], F32)
        b1_pp = SB("b1_pp", [128, NHFC], F32)
        kap_pp = SB("kap_pp", [128, NEC], F32)
        xmean = SB("xmean", [128, NEC], F32)
        xpart2 = SB("xpart2", [128, NEC * NST], F32)
        xmean_b = SB("xmean_b", [128, NEC], BF16)
        bv_row = SB("bv_row", [1, E], BF16)

        tr.op("dve", lambda e: e.memset(ones_b1[:], 1.0))
        tr.op("dve", lambda e: e.memset(ones_f1[:], 1.0))
        tr.op("dve", lambda e: e.memset(ones32[:], 1.0))
        v_ones = tr.op("dve", lambda e: e.memset(ones_c1f[:], 1.0), sem=dve)
        v_id = tr.op("sp", lambda e: e.dma_start(id_f[:], idm_d), sem="s_id", n=16)
        v_bv = tr.op("gp", lambda e: e.dma_start(bv_row[:], bv_d), sem="g_bv", n=16)

        with ExitStack() as stA:
            SBA = lambda *a: stA.enter_context(nc.sbuf_tensor(*a))
            PSA = lambda *a: stA.enter_context(nc.psum_tensor(*a))

            xt_bf = [SBA(f"xt_bf{i}", [128, S], BF16) for i in range(NEC)]
            wq_sb = SBA("wq_sb", [128, NEC * E], BF16)
            wk_sb = SBA("wk_sb", [128, NEC * E], BF16)
            wv_sb = SBA("wv_sb", [128, NEC * E], BF16)
            wcast = [SBA(f"wcast{i}", [128, HF], BF16) for i in range(2)]
            xnat = [SBA(f"xnat{i}", [128, 4 * E], F32) for i in range(2)]
            xtf_b = [SBA(f"xtf_b{i}", [128, 512], F32) for i in range(2)]
            q_b = [SBA(f"q_b{i}", [128, S], BF16) for i in range(2)]
            v_b = [SBA(f"v_b{i}", [128, E], BF16) for i in range(2)]
            brow = SBA("brow", [128, HF], F32)

            pst = [PSA(f"pst{i}", [128, 512], F32) for i in range(2)]
            psq = [PSA(f"psq{i}", [128, 512], F32) for i in range(2)]
            psv = [PSA(f"psv{i}", [128, E], F32) for i in range(2)]

            tr.wait("pe", dve, v_ones)
            tr.wait("pe", "s_id", v_id)
            last_bias_dve = 0
            for bd, pp_t, nch in ((bq_d, bq_pp, NEC), (bk_d, bk_pp, NEC),
                                  (bo_d, bo_pp, NEC), (b2_d, b2_pp, NEC),
                                  (b1_d, b1_pp, NHFC)):
                tr.wait("sp", dve, last_bias_dve)
                v_ld = tr.op("sp", lambda e, bd=bd, n=nch: e.dma_start(
                    brow[0:1, 0:128 * n], bd), sem="s_brow", n=16)
                tr.wait("pe", "s_brow", v_ld)
                for j4 in range(0, nch, 4):
                    nb = min(4, nch - j4)
                    v_tp = None
                    for jj in range(nb):
                        j = j4 + jj
                        fn = lambda e, j=j, jj=jj: e.transpose(
                            pst[0][:, jj * 128:(jj + 1) * 128],
                            brow[:, j * 128:(j + 1) * 128], id_f[:])
                        v_tp = tr.op("pe", fn, sem=(pe if jj == nb - 1 else None))
                    tr.wait("dve", pe, v_tp)
                    last_bias_dve = tr.op(
                        "dve", lambda e, pp_t=pp_t, j4=j4, nb=nb: e.tensor_copy(
                            pp_t[:, j4:j4 + nb],
                            bass.AP(pst[0], 0, [[512, 128], [128, nb]])),
                        sem=dve)
                    tr.wait("pe", dve, last_bias_dve)

            v_wc = 0
            wcast_list = () if alvl < 2 else ((wq_d, wq_sb), (wk_d, wk_sb), (wv_d, wv_sb))
            for w_src, w_dst in wcast_list:
                v_wc = tr.op("gp", lambda e, s=w_src, d=w_dst: e.dma_start(
                    bass.AP(d, 0, [[NEC * E, 128], [E, NEC], [1, E]]),
                    bass.AP(s.tensor, 0, [[E, 128], [128 * E, NEC], [1, E]])),
                    sem="g_wc", n=16)

            def emit_jobs():
                prev_wr = {0: 0, 1: 0}
                jobs = ([("w1", cc, w1_d[cc * 128:(cc + 1) * 128, :], HF) for cc in range(NEC)]
                    + [("w2", hc, w2_d[hc * 128:(hc + 1) * 128, :], E) for hc in range(NHFC)]
                    + [("wo", cc, wo_d[cc * 128:(cc + 1) * 128, :], E) for cc in range(NEC)])
                if alvl < 3:
                    jobs = []
                v_scrw = 0
                for ji, (kind, idx, src, width) in enumerate(jobs):
                    buf = ji % 2
                    tr.wait("gp", f"s_jw{buf}", prev_wr[buf])
                    v_cast = tr.op("gp", lambda e, b=buf, s=src, w=width: e.dma_start(
                        wcast[b][:, 0:w], s), sem=f"g_jc{buf}", n=16)
                    tr.wait("sp", f"g_jc{buf}", v_cast)
                    if kind == "w1":
                        for h8 in range(0, NHFC, 8):
                            d8 = bass.AP(w1s_d.tensor,
                                         (h8 * NEC + idx) * 128 * 128,
                                         [[128, 128], [NEC * 128 * 128, 8], [1, 128]])
                            s8 = bass.AP(wcast[buf], h8 * 128,
                                         [[HF, 128], [128, 8], [1, 128]])
                            v_scrw = tr.op("sp", lambda e, d8=d8, s8=s8: e.dma_start(
                                d8, s8), sem=f"s_jw{buf}", n=16)
                    elif kind == "w2":
                        for m8 in range(0, NMC, 8):
                            d8 = bass.AP(w2s_d.tensor,
                                         (m8 * NHFC + idx) * 128 * 128,
                                         [[128, 128], [NHFC * 128 * 128, 8], [1, 128]])
                            s8 = bass.AP(wcast[buf], m8 * 128,
                                         [[HF, 128], [128, 8], [1, 128]])
                            v_scrw = tr.op("sp", lambda e, d8=d8, s8=s8: e.dma_start(
                                d8, s8), sem=f"s_jw{buf}", n=16)
                    else:
                        d8 = bass.AP(wos_d.tensor, idx * 128 * 128,
                                     [[128, 128], [NEC * 128 * 128, NMC], [1, 128]])
                        s8 = bass.AP(wcast[buf], 0,
                                     [[HF, 128], [128, NMC], [1, 128]])
                        v_scrw = tr.op("sp", lambda e, d8=d8, s8=s8: e.dma_start(
                            d8, s8), sem=f"s_jw{buf}", n=16)
                    prev_wr[buf] = v_scrw
                    yield
            jobs_gen = emit_jobs()
            stripe_pe_done = [0, 0]
            pst_free_d = [0, 0]
            pst_free_a = [0, 0]
            psq_free = [0, 0]
            psv_free = [0, 0]
            qk_buf_free = [0, 0]
            v_buf_free = [0, 0]
            xtf_free = [0, 0]

            xpart = xmean
            v_xtbf_l = [0] * (NST + 1)

            v_xld_l = [0] * (NST + 1)

            def emit_xload(stp):
                buf = stp % 2
                tr.wait("sp", pe, stripe_pe_done[buf])
                if "L" in sparts:
                    v_xld_l[stp] = tr.op("sp", lambda e, stp=stp, b=buf: e.dma_start(
                        bass.AP(xnat[b], 0, [[4 * E, 128], [E, 4], [1, E]]),
                        bass.AP(x_d.tensor, stp * 512 * E,
                                [[E, 128], [128 * E, 4], [1, E]])),
                        sem=f"s_x{buf}", n=16)

            def emit_transposes(stp):
                buf = stp % 2
                tr.wait("pe", f"s_x{buf}", v_xld_l[stp])
                for ec in range(NEC if "T" in sparts else 0):
                    tb = ec % 2
                    tr.wait("pe", act, pst_free_a[tb])
                    v_tp = None
                    for j in range(4):
                        fn = lambda e, b=buf, ec=ec, j=j, tb=tb: e.transpose(
                            pst[tb][:, j * 128:(j + 1) * 128],
                            xnat[b][:, j * E + ec * 128: j * E + (ec + 1) * 128],
                            id_f[:])
                        v_tp = tr.op("pe", fn, sem=(pe if j == 3 else None))
                    if "V" in sparts:
                        tr.wait("act", pe, v_tp)
                        tr.wait("act", f"s_xtf{tb}", xtf_free[tb])
                        va = tr.op("act", lambda e, tb=tb: e.copy(
                            xtf_b[tb][:], pst[tb][:]))
                        pst_free_a[tb] = va
                        if "A" in sparts:
                            tr.wait("dve", act, va)
                            vv = tr.op("dve", lambda e, ec=ec, stp=stp, tb=tb: e.tensor_copy(
                                xt_bf[ec][:, stp * 512:(stp + 1) * 512],
                                xtf_b[tb][:]), sem=dve)
                        else:
                            vv = tr.val(dve)
                        if alvl >= 5 and sub4 >= "b":
                            tr.op("dve", lambda e, ec=ec, stp=stp: e.reduce_sum(
                                xpart2[:, ec * NST + stp: ec * NST + stp + 1],
                                xt_bf[ec][:, stp * 512:(stp + 1) * 512], axis=AXX))
                        tr.wait("sp", act, va)
                        tr.wait("sp", dve, vv)
                        xtf_free[tb] = tr.op("sp", lambda e, ec=ec, stp=stp, tb=tb: e.dma_start(
                            xt_d[ec * 128:(ec + 1) * 128, stp * 512:(stp + 1) * 512],
                            xtf_b[tb][:]), sem=f"s_xtf{tb}", n=16)
                    else:
                        pst_free_a[tb] = tr.val(act)
                        vv = tr.val(dve)
                v_xtbf_l[stp] = tr.val(dve)
                stripe_pe_done[buf] = tr.val(pe)

            def emit_qv(stp):
                if sub4 >= "b":
                    if stp == 0:
                        tr.wait("pe", "g_wc", v_wc)
                        tr.wait("pe", "g_bv", v_bv)
                    tr.wait("pe", dve, v_xtbf_l[stp])
                sb_big = q_b[stp % 2]
                if sub4 >= "b":
                    tr.wait("dve", f"s_qk{stp % 2}", qk_buf_free[stp % 2])
                vv = 0
                for fc in range(NEC if sub4 >= "b" else 0):
                    qb = fc % 2
                    tr.wait("pe", dve, psq_free[qb])
                    v_mm = None
                    for cc in range(NEC):
                        fn = lambda e, fc=fc, cc=cc, stp=stp, qb=qb: e.matmul(
                            psq[qb][:],
                            wq_sb[:, cc * E + fc * 128: cc * E + (fc + 1) * 128],
                            xt_bf[cc][:, stp * 512:(stp + 1) * 512],
                            start=(cc == 0), stop=(cc == NEC - 1))
                        v_mm = tr.op("pe", fn, sem=(pe if cc == NEC - 1 else None))
                    tr.wait("dve", pe, v_mm)
                    vv = tr.op("dve", lambda e, fc=fc, qb=qb, sb_big=sb_big: e.tensor_scalar(
                        sb_big[:, fc * 512:(fc + 1) * 512], psq[qb][:],
                        bq_pp[:, fc:fc + 1], None, ALU.add), sem=dve)
                    psq_free[qb] = vv
                if sub4 >= "b":
                    tr.wait("sp", dve, vv)
                    qk_buf_free[stp % 2] = tr.op(
                        "sp", lambda e, stp=stp, sb_big=sb_big: e.dma_start(
                            bass.AP(q_d.tensor, stp * 512,
                                    [[S, 128], [128 * S, NEC], [1, 512]]),
                            bass.AP(sb_big, 0, [[S, 128], [512, NEC], [1, 512]])),
                        sem=f"s_qk{stp % 2}", n=16)

                for j in range(4 if sub4 >= "c" else 0):
                    vb = j % 2
                    kc = stp * 4 + j
                    tr.wait("pe", dve, psv_free[vb])
                    v_mm = None
                    for half in range(2):
                        for cc in range(NEC):
                            tr.op("pe", lambda e, j=j, cc=cc, stp=stp, vb=vb, half=half: e.matmul(
                                psv[vb][:, half * 512:(half + 1) * 512],
                                xt_bf[cc][:, stp * 512 + j * 128: stp * 512 + (j + 1) * 128],
                                wv_sb[:, cc * E + half * 512: cc * E + (half + 1) * 512],
                                start=(cc == 0), stop=False))
                        fn = lambda e, vb=vb, half=half: e.matmul(
                            psv[vb][:, half * 512:(half + 1) * 512],
                            ones_b1[:], bv_row[:, half * 512:(half + 1) * 512],
                            start=False, stop=True)
                        v_mm = tr.op("pe", fn, sem=(pe if half == 1 else None))
                    tr.wait("dve", pe, v_mm)
                    tr.wait("dve", f"g_vst{vb}", v_buf_free[vb])
                    vv = tr.op("dve", lambda e, vb=vb: e.tensor_copy(
                        v_b[vb][:], psv[vb][:]), sem=dve)
                    psv_free[vb] = vv
                    tr.wait("gp", dve, vv)
                    v_buf_free[vb] = tr.op("gp", lambda e, kc=kc, vb=vb: e.dma_start(
                        bass.AP(v_d.tensor, kc * 128,
                                [[S, 128], [128 * S, NP], [1, 128]]),
                        bass.AP(v_b[vb], 0, [[E, 128], [128, NP], [1, 128]])),
                        sem=f"g_vst{vb}", n=16)

            nst_eff = nst if alvl >= 4 else 0
            for stp in range(nst_eff + 1):
                if stp < nst_eff:
                    emit_xload(stp)
                if stp >= 1:
                    emit_qv(stp - 1)
                    for _ in range(3):
                        try:
                            next(jobs_gen)
                        except StopIteration:
                            break
                if stp < nst_eff:
                    emit_transposes(stp)
                    for _ in range(4):
                        try:
                            next(jobs_gen)
                        except StopIteration:
                            break

            v_xm = 0
            if alvl >= 5:
                for ec in range(NEC):
                    tr.op("dve", lambda e, ec=ec: e.reduce_sum(
                        xmean[:, ec:ec + 1],
                        xpart2[:, ec * NST:(ec + 1) * NST], axis=AXX))
                v_xm = tr.op("dve", lambda e: e.tensor_scalar(
                    xmean_b[:], xmean[:], 1.0 / S, None, ALU.mult), sem=dve)

            tr.wait("pe", dve, v_xm)
            tr.wait("pe", act, pst_free_a[0])
            v_kap = 0
            for fc in range(NEC if alvl >= 5 else 0):
                v_mm = None
                for cc in range(NEC):
                    fn = lambda e, fc=fc, cc=cc: e.matmul(
                        psq[0][:, 0:1],
                        wk_sb[:, cc * E + fc * 128: cc * E + (fc + 1) * 128],
                        xmean_b[:, cc:cc + 1],
                        start=(cc == 0), stop=(cc == NEC - 1))
                    v_mm = tr.op("pe", fn, sem=(pe if cc == NEC - 1 else None))
                tr.wait("dve", pe, v_mm)
                v_kap = tr.op("dve", lambda e, fc=fc: e.tensor_scalar(
                    kap_pp[:, fc:fc + 1], psq[0][:, 0:1], bk_pp[:, fc:fc + 1],
                    None, ALU.add), sem=dve)
                tr.wait("pe", dve, v_kap)

            for stp in range(NST if alvl >= 5 else 0):
                sb_big = q_b[stp % 2]
                tr.wait("dve", f"s_qk{stp % 2}", qk_buf_free[stp % 2])
                vv = 0
                for fc in range(NEC):
                    qb = fc % 2
                    tr.wait("pe", dve, psq_free[qb])
                    v_mm = None
                    for cc in range(NEC):
                        fn = lambda e, fc=fc, cc=cc, stp=stp, qb=qb: e.matmul(
                            psq[qb][:],
                            wk_sb[:, cc * E + fc * 128: cc * E + (fc + 1) * 128],
                            xt_bf[cc][:, stp * 512:(stp + 1) * 512],
                            start=(cc == 0), stop=(cc == NEC - 1))
                        v_mm = tr.op("pe", fn, sem=(pe if cc == NEC - 1 else None))
                    tr.wait("dve", pe, v_mm)
                    vv = tr.op("dve", lambda e, fc=fc, qb=qb, sb_big=sb_big: e.tensor_scalar(
                        sb_big[:, fc * 512:(fc + 1) * 512], psq[qb][:],
                        bk_pp[:, fc:fc + 1],
                        kap_pp[:, fc:fc + 1], ALU.add, ALU.subtract), sem=dve)
                    psq_free[qb] = vv
                tr.wait("sp", dve, vv)
                qk_buf_free[stp % 2] = tr.op(
                    "sp", lambda e, stp=stp, sb_big=sb_big: e.dma_start(
                        bass.AP(k_d.tensor, stp * 512,
                                [[S, 128], [128 * S, NEC], [1, 512]]),
                        bass.AP(sb_big, 0, [[S, 128], [512, NEC], [1, 512]])),
                    sem=f"s_qk{stp % 2}", n=16)

            for _ in jobs_gen:
                pass
            A_pe, A_act, A_dve = tr.val(pe), tr.val(act), tr.val(dve)
            A_finals = {s: tr.val(s) for s in ("s_qk0", "s_qk1", "s_xtf0",
                                               "s_xtf1", "g_vst0", "g_vst1",
                                               "s_jw0", "s_jw1", "s_x0",
                                               "s_x1", "s_brow", "g_wc")
                        if s in tr.semv}

        k_sb = [SB(f"k_sb{i}", [128, S], BF16) for i in range(2)]
        v_sb = [SB(f"v_sb{i}", [128, S], BF16) for i in range(2)]
        q_sb = SB("q_sb", [128, S], BF16)
        xt_sb = [SB(f"xt_sb{i}", [128, 512], F32) for i in range(2)]
        wo_sb = [SB(f"wo_sb{i}", [128, E], BF16) for i in range(2)]
        w1_sb = [SB(f"w1_sb{i}", [128, E], BF16) for i in range(3)]
        w2_sb = [SB(f"w2_sb{i}", [128, HF // 2], BF16) for i in range(3)]
        es_sb = [SB(f"es_sb{i}", [128, 1024], BF16) for i in range(2)]
        o_sb = [SB(f"o_sb{i}", [128, S], BF16) for i in range(2)]
        hpre_sb = SB("hpre_sb", [128, 16 * 512], BF16)
        ones64 = SB("ones64", [128, 64], BF16)
        y1_sb = SB("y1_sb", [128, S], F32)
        h1b_sb = SB("h1b_sb", [128, S], BF16)
        h1f_sb = SB("h1f_sb", [128, S], F32)
        h2_sb = SB("h2_sb", [128, NHFC * 512], BF16)
        out_sb = [SB("out_sb0", [128, E], F32)]
        su_sb = [SB(f"su_sb{i}", [128, 1024], F32) for i in range(2)]
        sq_sb = SB("sq_sb", [128, 512], F32)
        stA = SB("stA", [128, 512], F32)
        stB = SB("stB", [128, 512], F32)
        rb_sb = SB("rb_sb", [128, 512], F32)
        mrb_sb = SB("mrb_sb", [128, 512], F32)
        r_sb = SB("r_sb", [128, 512], F32)
        dD_sb = SB("dD_sb", [128, 512], F32)
        oO_sb = SB("oO_sb", [128, 512], F32)

        ps_s = [PS("ps_s0", [128, 1024], F32), PS("ps_s1", [128, 1024], F32)]
        ps_o = PS("ps_o", [128, 512], F32)
        ps_d = PS("ps_d", [128, 512], F32)
        ps_g = PS("ps_g", [128, 512], F32)
        ps_w = PS("ps_w", [128, 512], F32)
        pwb = [ps_w, ps_g]

        v_ones64 = tr.op("dve", lambda e: e.memset(ones64[:], 1.0))

        for q_ in ("sp", "gp"):
            tr.wait(q_, pe, A_pe)
            tr.wait(q_, act, A_act)
            tr.wait(q_, dve, A_dve)
            for s_, v_ in A_finals.items():
                tr.wait(q_, s_, v_)

        free_k = [0, 0]
        free_v = [0, 0]
        free_q = 0
        free_xt = [0, 0]
        free_wo = [0, 0]
        free_w1 = [0, 0, 0]
        free_w2 = [0, 0, 0]
        free_out = [0]
        v_k = [0, 0]
        v_v = [0, 0]
        v_q = 0
        v_xt = [0, 0]
        v_wo = [0, 0]
        v_w1 = [0, 0, 0]
        v_w2 = [0, 0, 0]
        es_free = {}
        norm_done = 0
        o_last = 0
        o_ready = {}
        last_pw = [0, 0]
        ln_app = 0
        gelu_batch = {}
        qc_range = range(nqc) if "B" in phases else range(0)

        def load_kv(p):
            b = p % 2
            tr.wait("gp", pe, free_k[b])
            v_k[b] = tr.op("gp", lambda e, p=p, b=b: e.dma_start(
                k_sb[b][:], bass.AP(k_d.tensor, p * 128 * S, [[S, 128], [1, S]])),
                sem=f"s_k{b}", n=16)
            tr.wait("gp", pe, free_v[b])
            v_v[b] = tr.op("gp", lambda e, p=p, b=b: e.dma_start(
                v_sb[b][:], bass.AP(v_d.tensor, p * 128 * S, [[S, 128], [1, S]])),
                sem=f"s_v{b}", n=16)

        def stats_chunk(src_sb, c, st):
            tr.op("dve", lambda e, c=c: e.tensor_tensor(
                sq_sb[:], src_sb[:, c * 512:(c + 1) * 512],
                src_sb[:, c * 512:(c + 1) * 512], ALU.mult))
            if c == 0:
                tr.op("dve", lambda e: e.tensor_copy(
                    su_sb[0][:, 0:512], src_sb[:, 0:512]))
                st["v"] = tr.op("dve", lambda e: e.tensor_copy(
                    su_sb[0][:, 512:1024], sq_sb[:]))
                st["pp"] = 0
            else:
                pp = st["pp"]
                np_ = 1 - pp
                tr.op("dve", lambda e, c=c, pp=pp, np_=np_: e.tensor_tensor(
                    su_sb[np_][:, 0:512], su_sb[pp][:, 0:512],
                    src_sb[:, c * 512:(c + 1) * 512], ALU.add))
                st["v"] = tr.op("dve", lambda e, pp=pp, np_=np_: e.tensor_tensor(
                    su_sb[np_][:, 512:1024], su_sb[pp][:, 512:1024],
                    sq_sb[:], ALU.add))
                st["pp"] = np_

        def emit_ln_tail(src_sb, dst_f, dst_b, st):
            nonlocal ln_app
            pp = st["pp"]
            v_sums = st["v"]
            tr.wait("pe", dve, max(v_sums, last_pw[0], last_pw[1]))
            tr.op("pe", lambda e, pp=pp: e.matmul(
                pwb[0][0:1, :], ones_c1f[:], su_sb[pp][:, 0:512],
                start=True, stop=True))
            v_st = tr.op("pe", lambda e, pp=pp: e.matmul(
                pwb[1][0:1, :], ones_c1f[:], su_sb[pp][:, 512:1024],
                start=True, stop=True), sem=pe)
            tr.wait("dve", pe, v_st)
            last_pw[0] = tr.op("dve", lambda e: e.tensor_copy(stA[0:1, :], pwb[0][0:1, :]))
            last_pw[1] = tr.op("dve", lambda e: e.tensor_copy(stA[32:33, :], pwb[1][0:1, :]))
            tr.op("dve", lambda e: e.tensor_scalar(
                stB[0:1, :], stA[0:1, :], 1.0 / E, None, ALU.mult))
            tr.op("dve", lambda e: e.tensor_tensor(
                stB[32:33, :], stB[0:1, :], stB[0:1, :], ALU.mult))
            tr.op("dve", lambda e: e.scalar_tensor_tensor(
                stA[64:65, :], stA[32:33, :], 1.0 / E,
                stB[32:33, :], ALU.mult, ALU.subtract))
            v_var = tr.op("dve", lambda e: e.tensor_scalar(
                stA[96:97, :], stA[64:65, :], EPS, None, ALU.add))
            yield
            yield
            tr.wait("act", dve, v_var)
            tr.op("act", lambda e: e.activation(
                stB[96:97, :], stA[96:97, :], AF.Ln))
            v_rs = tr.op("act", lambda e: e.activation(
                stA[0:1, :], stB[96:97, :], AF.Exp, scale=-0.5))
            tr.wait("dve", act, v_rs)
            v_mr = tr.op("dve", lambda e: e.tensor_tensor(
                sq_sb[0:1, :], stB[0:1, :], stA[0:1, :], ALU.mult))
            yield
            yield
            tr.wait("pe", dve, max(v_mr, last_pw[0], last_pw[1]))
            tr.op("pe", lambda e: e.matmul(
                pwb[0][:], ones_f1[:], stA[0:1, :], start=True, stop=True))
            v_bc = tr.op("pe", lambda e: e.matmul(
                pwb[1][:], ones_f1[:], sq_sb[0:1, :], start=True, stop=True), sem=pe)
            tr.wait("dve", pe, v_bc)
            last_pw[0] = tr.op("dve", lambda e: e.tensor_copy(rb_sb[:], pwb[0][:]))
            last_pw[1] = tr.op("dve", lambda e: e.tensor_copy(mrb_sb[:], pwb[1][:]))
            yield
            yield
            for c in range(NMC):
                tr.op("dve", lambda e, c=c: e.tensor_tensor(
                    sq_sb[:], src_sb[:, c * 512:(c + 1) * 512], rb_sb[:],
                    ALU.mult))
                if dst_f is not None:
                    tr.op("dve", lambda e, c=c: e.tensor_tensor(
                        dst_f[:, c * 512:(c + 1) * 512], sq_sb[:], mrb_sb[:],
                        ALU.subtract))
                    ln_app = tr.op("dve", lambda e, c=c: e.tensor_copy(
                        dst_b[:, c * 512:(c + 1) * 512],
                        dst_f[:, c * 512:(c + 1) * 512]))
                else:
                    ln_app = tr.op("dve", lambda e, c=c: e.tensor_tensor(
                        dst_b[:, c * 512:(c + 1) * 512], sq_sb[:], mrb_sb[:],
                        ALU.subtract))
                yield

        def make_post(pq):
            nonlocal ln_app, norm_done
            st1 = {"pp": 0, "v": 0}
            st2 = {"pp": 0, "v": 0}
            osrc = o_sb[pq % 2]
            for mc in range(NMC):
                wb = mc % 2
                xb = mc % 2
                tr.wait("gp", dve, free_xt[xb])
                v_xt[xb] = tr.op("gp", lambda e, mc=mc, pq=pq, xb=xb: e.dma_start(
                    xt_sb[xb][:],
                    bass.AP(xt_d.tensor, mc * 128 * S + pq * 512, [[S, 128], [1, 512]])),
                    sem=f"s_xt{xb}", n=16)
                tr.wait("sp", pe, free_wo[wb])
                v_wo[wb] = tr.op("sp", lambda e, mc=mc, wb=wb: e.dma_start(
                    bass.AP(wo_sb[wb], 0, [[E, 128], [128, NEC], [1, 128]]),
                    bass.AP(wos_d.tensor, mc * NEC * 128 * 128,
                            [[128, 128], [128 * 128, NEC], [1, 128]])),
                    sem=f"s_wo{wb}", n=16)
                tr.wait("pe", f"s_wo{wb}", v_wo[wb])
                pb = mc % 2
                tr.wait("pe", dve, max(last_pw[pb], o_ready.get(pq, 0)))
                v_mm = None
                for cc in range(NEC):
                    fn = lambda e, cc=cc, wb=wb, pb=pb: e.matmul(
                        pwb[pb][:], wo_sb[wb][:, cc * 128:(cc + 1) * 128],
                        osrc[:, cc * 512:(cc + 1) * 512],
                        start=(cc == 0), stop=(cc == NEC - 1))
                    v_mm = tr.op("pe", fn, sem=(pe if cc == NEC - 1 else None))
                free_wo[wb] = v_mm
                tr.wait("dve", pe, v_mm)
                tr.wait("dve", f"s_xt{xb}", v_xt[xb])
                last_pw[pb] = tr.op("dve", lambda e, mc=mc, xb=xb, pb=pb: e.scalar_tensor_tensor(
                    y1_sb[:, mc * 512:(mc + 1) * 512], pwb[pb][:],
                    bo_pp[:, mc:mc + 1], xt_sb[xb][:],
                    ALU.add, ALU.add))
                free_xt[xb] = last_pw[pb]
                stats_chunk(y1_sb, mc, st1)
                yield
            yield from emit_ln_tail(y1_sb, h1f_sb, h1b_sb, st1)
            yield
            yield
            yield
            yield
            v_ln1 = ln_app
            tr.wait("pe", dve, v_ln1)
            for hc in range(NHFC):
                wb = hc % 3
                pb = hc % 2
                tr.wait("sp", pe, free_w1[wb])
                v_w1[wb] = tr.op("sp", lambda e, hc=hc, wb=wb: e.dma_start(
                    bass.AP(w1_sb[wb], 0, [[E, 128], [128, NEC], [1, 128]]),
                    bass.AP(w1s_d.tensor, hc * NEC * 128 * 128,
                            [[128, 128], [128 * 128, NEC], [1, 128]])),
                    sem=f"s_w1{wb}", n=16)
                tr.wait("pe", f"s_w1{wb}", v_w1[wb])
                tr.wait("pe", dve, last_pw[pb])
                v_mm = None
                for cc in range(NEC):
                    fn = lambda e, cc=cc, wb=wb, pb=pb: e.matmul(
                        pwb[pb][:], w1_sb[wb][:, cc * 128:(cc + 1) * 128],
                        h1b_sb[:, cc * 512:(cc + 1) * 512],
                        start=(cc == 0), stop=(cc == NEC - 1))
                    v_mm = tr.op("pe", fn, sem=(pe if cc == NEC - 1 else None))
                free_w1[wb] = v_mm
                tr.wait("dve", pe, v_mm)
                if hc < 16:
                    tr.wait("dve", act, gelu_batch.get((pq - 1, 1), 0))
                else:
                    tr.wait("dve", act, gelu_batch.get((pq, 0), 0))
                last_pw[pb] = tr.op("dve", lambda e, hc=hc, pb=pb: e.tensor_scalar(
                    hpre_sb[:, (hc % 16) * 512:(hc % 16 + 1) * 512], pwb[pb][:],
                    b1_pp[:, hc:hc + 1], None, ALU.add))
                if hc % 16 == 15:
                    bi = hc // 16
                    tr.wait("act", dve, max(last_pw[0], last_pw[1]))
                    gelu_batch[(pq, bi)] = tr.op(
                        "act", lambda e, hc=hc: e.activation(
                            h2_sb[:, (hc - 15) * 512:(hc + 1) * 512],
                            hpre_sb[:], AF.Gelu))
                yield
            v_gelu = gelu_batch[(pq, 1)]
            ffn2_ev = {}
            for mc in range(NMC):
                bufs = [(2 * mc) % 3, (2 * mc + 1) % 3]
                for half in range(2):
                    wb = bufs[half]
                    tr.wait("sp", pe, free_w2[wb])
                    v_w2[wb] = tr.op("sp", lambda e, mc=mc, wb=wb, half=half: e.dma_start(
                        bass.AP(w2_sb[wb], 0, [[HF // 2, 128], [128, 16], [1, 128]]),
                        bass.AP(w2s_d.tensor, (mc * NHFC + half * 16) * 128 * 128,
                                [[128, 128], [128 * 128, 16], [1, 128]])),
                        sem=f"s_w2{wb}", n=16)
                tr.wait("pe", act, v_gelu)
                pb = mc % 2
                tr.wait("pe", dve, last_pw[pb])
                v_mm = None
                for sub in range(4):
                    wb = bufs[sub // 2]
                    if sub % 2 == 0:
                        tr.wait("pe", f"s_w2{wb}", v_w2[wb])
                    for hi in range(8):
                        hc = sub * 8 + hi
                        fn = lambda e, hc=hc, wb=wb, pb=pb: e.matmul(
                            pwb[pb][:], w2_sb[wb][:, (hc % 16) * 128:(hc % 16 + 1) * 128],
                            h2_sb[:, hc * 512:(hc + 1) * 512],
                            start=(hc == 0), stop=(hc == NHFC - 1))
                        v_mm = tr.op("pe", fn,
                                     sem=(pe if hc == NHFC - 1 or hi == 7 else None))
                    if sub == 1:
                        free_w2[bufs[0]] = v_mm
                    if sub < 3:
                        yield
                free_w2[bufs[1]] = v_mm
                tr.wait("dve", pe, v_mm)
                last_pw[pb] = tr.op("dve", lambda e, mc=mc, pb=pb: e.scalar_tensor_tensor(
                    y1_sb[:, mc * 512:(mc + 1) * 512], pwb[pb][:],
                    b2_pp[:, mc:mc + 1], h1f_sb[:, mc * 512:(mc + 1) * 512],
                    ALU.add, ALU.add))
                stats_chunk(y1_sb, mc, st2)
                yield
            yield from emit_ln_tail(y1_sb, None, h1f_sb, st2)
            yield
            v_ln2 = ln_app
            for ts_ in range(4):
                for half in range(2):
                    tr.wait("pe", dve, max(v_ln2, last_pw[half]))
                    if half == 0:
                        tr.wait("pe", "g_out", free_out[0])
                    v_tp = None
                    for mi in range(4):
                        mc = half * 4 + mi
                        fn = lambda e, ts_=ts_, mc=mc, mi=mi, half=half: e.transpose(
                            pwb[half][:, mi * 128:(mi + 1) * 128],
                            h1f_sb[:, mc * 512 + ts_ * 128: mc * 512 + (ts_ + 1) * 128],
                            id_f[:])
                        v_tp = tr.op("pe", fn, sem=(pe if mi == 3 else None))
                    tr.wait("dve", pe, v_tp)
                    last_pw[half] = tr.op("dve", lambda e, half=half: e.tensor_copy(
                        out_sb[0][:, half * 512:(half + 1) * 512], pwb[half][:]))
                tr.wait("sp", dve, max(last_pw[0], last_pw[1]))
                free_out[0] = tr.op("sp", lambda e, ts_=ts_, pq=pq: e.dma_start(
                    y_d[pq * 512 + ts_ * 128: pq * 512 + (ts_ + 1) * 128, :],
                    out_sb[0][:]), sem="g_out", n=16)
                yield

        post_gen = None

        def pump(n=1):
            nonlocal post_gen
            for _ in range(n):
                if post_gen is None:
                    return
                try:
                    next(post_gen)
                except StopIteration:
                    post_gen = None
                    return

        def load_q(qc):
            nonlocal v_q
            tr.wait("gp", pe, free_q)
            for pc in range(NP):
                v_q = tr.op("gp", lambda e, pc=pc, qc=qc: e.dma_start(
                    q_sb[:, pc * 512:(pc + 1) * 512],
                    bass.AP(q_d.tensor, pc * 128 * S + qc * 512, [[S, 128], [1, 512]])),
                    sem="s_qb", n=16)

        for qc in qc_range:
            post_gen = make_post(qc - 1) if qc > 0 else None
            if qc == 0:
                tr.wait("pe", dve, v_ones64)
                load_q(0)
                load_kv(0)
                load_kv(1)
            for p in range(NP):
                b = p % 2
                if 1 <= p < NP - 1:
                    load_kv(p + 1)
                elif p == NP - 1 and qc + 1 in qc_range:
                    load_kv(0)
                sc_done = {}
                exp_done = {}

                def scores(kc, p=p, b=b):
                    if kc == 0:
                        tr.wait("pe", f"s_k{b}", v_k[b])
                        tr.wait("pe", "s_qb", v_q)
                    v = None
                    for hh in range(2):
                        fn = lambda e, p=p, b=b, kc=kc, hh=hh: e.matmul(
                            ps_s[kc % 2][:, hh * 512:(hh + 1) * 512],
                            k_sb[b][hh * 64:(hh + 1) * 64, kc * 128:(kc + 1) * 128],
                            q_sb[hh * 64:(hh + 1) * 64, p * 512:(p + 1) * 512],
                            start=True, stop=True, tile_position=(hh * 64, 0))
                        v = tr.op("pe", fn, sem=(pe if hh == 1 else None))
                    sc_done[kc] = v

                def do_exp(kc):
                    esb = kc % 2
                    tr.wait("act", pe, es_free.get(esb, 0))
                    tr.wait("act", pe, sc_done[kc])
                    exp_done[kc] = tr.op(
                        "act", lambda e, kc=kc, esb=esb: e.activation(
                            es_sb[esb][:], ps_s[kc % 2][:], AF.Exp, scale=SCALE))

                def pv(kc, p=p, b=b):
                    tr.wait("pe", act, exp_done[kc])
                    if kc == 0:
                        tr.wait("pe", f"s_v{b}", v_v[b])
                        tr.wait("pe", dve, norm_done)
                    esb = kc % 2
                    for hh in range(2):
                        tr.op("pe", lambda e, b=b, kc=kc, esb=esb, hh=hh: e.matmul(
                            ps_o[hh * 64:(hh + 1) * 64, :],
                            v_sb[b][:, kc * 128 + hh * 64: kc * 128 + (hh + 1) * 64],
                            es_sb[esb][:, hh * 512:(hh + 1) * 512],
                            start=(kc == 0), stop=(kc == NKC - 1),
                            skip_group_check=True,
                            tile_position=(0, hh * 64)))

                def denom(kc):
                    esb = kc % 2
                    v = None
                    for hh in range(2):
                        fn = lambda e, kc=kc, esb=esb, hh=hh: e.matmul(
                            ps_d[hh * 64:(hh + 1) * 64, :],
                            ones64[:],
                            es_sb[esb][:, hh * 512:(hh + 1) * 512],
                            start=(kc == 0), stop=(kc == NKC - 1),
                            skip_group_check=True,
                            tile_position=(0, hh * 64))
                        v = tr.op("pe", fn, sem=(pe if hh == 1 else None))
                    return v

                scores(0)
                do_exp(0)
                scores(1)
                vmm = 0
                for kc in range(NKC):
                    pv(kc)
                    vmm = denom(kc)
                    es_free[kc % 2] = vmm
                    if kc + 2 < NKC:
                        scores(kc + 2)
                    if kc + 1 < NKC:
                        do_exp(kc + 1)
                    if kc % 2 == 1:
                        pump(1)
                    if kc == 3:
                        pump(2)
                free_k[b] = vmm
                free_v[b] = vmm
                tr.wait("dve", pe, vmm)
                tr.op("dve", lambda e: e.tensor_copy(dD_sb[:], ps_d[:]))
                norm_done = tr.op("dve", lambda e: e.tensor_copy(oO_sb[:], ps_o[:]))
                tr.op("dve", lambda e: e.reciprocal(r_sb[:], dD_sb[:]))
                pcol = p * 512
                tr.op("dve", lambda e, pcol=pcol, qc=qc: e.tensor_tensor(
                    o_sb[qc % 2][0:64, pcol:pcol + 512],
                    oO_sb[0:64, :], r_sb[0:64, :], ALU.mult))
                o_last = tr.op("dve", lambda e, pcol=pcol, qc=qc: e.tensor_tensor(
                    o_sb[qc % 2][64:128, pcol:pcol + 512],
                    oO_sb[64:128, :], r_sb[64:128, :], ALU.mult))
                if p == NP - 1:
                    free_q = vmm
                pump(1)
            o_ready[qc] = o_last
            if qc + 1 in qc_range:
                load_kv(1)
                load_q(qc + 1)
            while post_gen is not None:
                pump(4)
        if "B" in phases and nqc > 0:
            post_gen = make_post(nqc - 1)
            while post_gen is not None:
                pump(8)

        tr.wait("sp", "g_out", tr.val("g_out") if "g_out" in tr.semv else 0)

        with nc.Block() as block:
            @block.tensor
            def _(eng):
                for f in tr.streams["pe"]:
                    f(eng)

            @block.scalar
            def _(eng):
                for f in tr.streams["act"]:
                    f(eng)

            @block.vector
            def _(eng):
                for f in tr.streams["dve"]:
                    f(eng)

            @block.sync
            def _(eng):
                for f in tr.streams["sp"]:
                    f(eng)

            @block.gpsimd
            def _(eng):
                for f in tr.streams["gp"]:
                    f(eng)

    return nc


_ID = np.eye(128, dtype=np.float32)


def kernel(L0_attn_x, L0_attn_Wq, L0_attn_bq, L0_attn_Wk, L0_attn_bk,
           L0_attn_Wv, L0_attn_bv, L0_attn_Wo, L0_attn_bo,
           L0_ffn_w1, L0_ffn_b1, L0_ffn_w2, L0_ffn_b2):
    x = np.ascontiguousarray(np.asarray(L0_attn_x, dtype=np.float32))
    f = lambda a: np.ascontiguousarray(np.asarray(a, dtype=np.float32))
    row = lambda a: f(a).reshape(1, -1)
    nc = build()
    shared = {
        "wq": f(L0_attn_Wq), "wk": f(L0_attn_Wk), "wv": f(L0_attn_Wv),
        "wo": f(L0_attn_Wo), "bq": row(L0_attn_bq), "bk": row(L0_attn_bk),
        "bv": row(L0_attn_bv), "bo": row(L0_attn_bo),
        "w1": f(L0_ffn_w1), "b1": row(L0_ffn_b1),
        "w2": f(L0_ffn_w2), "b2": row(L0_ffn_b2), "idm": _ID,
    }
    in_maps = [dict(shared, x=np.ascontiguousarray(x[c])) for c in range(8)]
    res = run_bass_kernel_spmd(nc, in_maps, core_ids=list(range(8)))
    return np.stack([np.asarray(r["y"]) for r in res.results], axis=0)
